# version 15 (final)
# speedup vs baseline: 1.1566x; 1.0015x over previous
import numpy as np
from contextlib import ExitStack

import concourse.bass as bass
import concourse.mybir as mybir
from concourse.bass import IndirectOffsetOnAxis
from concourse.bass_utils import run_bass_kernel_spmd

F32 = mybir.dt.float32
BF16 = mybir.dt.bfloat16
AF = mybir.ActivationFunctionType
ALU = mybir.AluOpType
AX = mybir.AxisListType
I32 = mybir.dt.int32
f32 = np.float32

NCORES = 8
D = 1024
KC = 8
HID = 1408
NH_A, NT_A = 640, 5
NH_B, NT_B = 768, 6
NE = 9
SEQ = 2048
NB = 2
T = NB * SEQ
NTILES = T // 128
HALF = 2048
HTILES = HALF // 128
CHUNK = 512
HCHUNKS = HALF // CHUNK


class Sem:
    __slots__ = ("name", "v", "h")

    def __init__(self, name):
        self.name, self.v, self.h = name, 0, None


class Plan:
    def __init__(self):
        self.ops = []

    def emit(self, engine, fn, waits=(), incs=()):
        ws = [(s, v) for (s, v) in waits if v > 0]
        ri = []
        for (s, k) in incs:
            s.v += k
            ri.append((s, k))
        self.ops.append((engine, ws, fn, ri))

    def run(self, engine_name, eng):
        for e, ws, fn, ri in self.ops:
            if e != engine_name:
                continue
            merged = {}
            for (s, v) in ws:
                if s.name in merged:
                    merged[s.name] = (s, max(v, merged[s.name][1]))
                else:
                    merged[s.name] = (s, v)
            for (s, v) in merged.values():
                eng.wait_ge(s.h, v)
            inst = fn(eng)
            for (s, k) in ri:
                inst.then_inc(s.h, k)


def build_nc(phases=("F1", "A", "F2"), dbg=False):
    nc = bass.Bass("TRN2", target_bir_lowering=False, debug=False)
    dbg_d = {}
    if dbg:
        dbg_d["yTh"] = nc.dram_tensor("dbg_yTh", [128, KC * HALF], F32, kind="ExternalOutput").ap()
        dbg_d["mTc"] = nc.dram_tensor("dbg_mTc", [128, 128], F32, kind="ExternalOutput").ap()
        dbg_d["dacc"] = nc.dram_tensor("dbg_dacc", [128, HTILES * D], F32, kind="ExternalOutput").ap()
        dbg_d["ye"] = nc.dram_tensor("dbg_ye", [128, KC * CHUNK], F32, kind="ExternalOutput").ap()
        dbg_d["xs0"] = nc.dram_tensor("dbg_xs0", [HALF, D], F32, kind="ExternalOutput").ap()
        dbg_d["dacc2"] = nc.dram_tensor("dbg_dacc2", [128, HTILES * D], F32, kind="ExternalOutput").ap()

    x_d = nc.dram_tensor("x", [T, D], F32, kind="ExternalInput").ap()
    wqkvT_d = nc.dram_tensor("wqkvT", [D, 192], F32, kind="ExternalInput").ap()
    woutT_d = nc.dram_tensor("woutT", [64, D], F32, kind="ExternalInput").ap()
    gwT_d = nc.dram_tensor("gwT", [D, 8], F32, kind="ExternalInput").ap()
    biasbc_d = nc.dram_tensor("biasbc", [128, 8], F32, kind="ExternalInput").ap()
    upT_d = nc.dram_tensor("upT", [NE, D, HID], F32, kind="ExternalInput").ap()
    dnT_d = nc.dram_tensor("dnT", [NE, HID, D], F32, kind="ExternalInput").ap()
    out_d = nc.dram_tensor("out", [T, D], F32, kind="ExternalOutput").ap()
    xs_d = nc.dram_tensor("xs", [T, D], F32, kind="Internal").ap()
    wsu_d = nc.dram_tensor("wsu_d", [NE * 128, KC * HID], BF16, kind="Internal").ap()
    wsd_d = nc.dram_tensor("wsd_d", [NE * 128, (HID // 128) * D], BF16, kind="Internal").ap()
    ynat_d = nc.dram_tensor("ynat_d", [T + 128, D], BF16, kind="Internal").ap()
    pairs_d = nc.dram_tensor("pairs_d", [10240, 2], F32, kind="Internal").ap()
    rbr_d = nc.dram_tensor("rbr_d", [8192, D], F32, kind="Internal").ap()
    rbs_d = nc.dram_tensor("rbs_d", [4096, D], F32, kind="Internal").ap()
    tt_hbm = nc.dram_tensor("tt_hbm", [2, 128], F32, kind="Internal").ap()
    po_hbm = nc.dram_tensor("po_hbm", [2, 160], F32, kind="Internal").ap()

    P = Plan()
    dInit = Sem("dInit")
    dRg = Sem("dRg")
    dYn = [Sem("dYn0"), Sem("dYn1")]
    dWu = [Sem("dWu0"), Sem("dWu1")]
    dWd = [Sem("dWd0"), Sem("dWd1")]
    dGe = [Sem("dGe0"), Sem("dGe1")]
    dPl = [Sem("dPl0"), Sem("dPl1")]
    dRb = [Sem("dRb0"), Sem("dRb1")]
    dSc = Sem("dSc")
    dTt = Sem("dTt")
    dPinit = Sem("dPinit")
    dB = [[Sem(f"dB{r}{i}") for i in range(3)] for r in range(2)]
    dXs = [Sem("dX0"), Sem("dX1")]
    dYs = [Sem("dY0"), Sem("dY1")]
    dM = [Sem("dM0"), Sem("dM1")]
    dR = [Sem("dR0"), Sem("dR1")]
    dG = [Sem(f"dG{i}") for i in range(4)]
    dDbg = Sem("dDbg")
    pe, act, vec, gp, bar = Sem("pe"), Sem("act"), Sem("vec"), Sem("gp"), Sem("bar")
    all_sems = [dInit, dRg, dYn[0], dYn[1], dWu[0], dWu[1], dWd[0], dWd[1],
                dGe[0], dGe[1], dPl[0], dPl[1], dRb[0], dRb[1], dSc, dTt, dPinit] + \
               dB[0] + dB[1] + [dXs[0], dXs[1], dYs[0], dYs[1],
                dM[0], dM[1], dR[0], dR[1]] + dG + [dDbg, pe, act, vec, gp, bar]

    def now(s):
        return (s, s.v)

    with ExitStack() as top:
        sb = lambda n_, sh, dt: top.enter_context(nc.sbuf_tensor(n_, sh, dt))
        ident = sb("ident", [128, 128], F32)
        ones1 = sb("ones1", [1, 128], BF16)
        iota8 = sb("iota8", [128, 8], F32)
        biasb = sb("biasb", [128, 8], F32)
        gw_sb = sb("gw_sb", [128, KC * 8], F32)
        xr = [sb(f"xr{i}", [128, D], F32) for i in range(2)]
        ybuf = [sb(f"ybuf{i}", [128, D], F32) for i in range(2)]
        stt_ = [sb(f"stt{i}", [128, 16], F32) for i in range(2)]
        identb = sb("identb", [128, 128], BF16)
        iotaP = sb("iotaP", [128, 1], F32)
        iotaW8 = sb("iotaW8", [128, 1], I32)
        thr8 = sb("thr8", [128, 8], F32)
        zb16 = sb("zb16", [128, D], BF16)
        PBbig = top.enter_context(nc.psum_tensor("PBbig", [128, 2048], F32))
        PB = [PBbig[:, 512 * i:512 * (i + 1)] for i in range(4)] + \
             [top.enter_context(nc.psum_tensor(f"PB{i}", [128, 512], F32)) for i in range(4, 8)]

        P.emit("gpsimd", lambda g: g.iota(ident[:, :], pattern=[[1, 128]], base=0,
                                          channel_multiplier=-1, allow_small_or_imprecise_dtypes=True),
               incs=[(gp, 1)])
        P.emit("gpsimd", lambda g: g.iota(iota8[:, :], pattern=[[1, 8]], base=0,
                                          channel_multiplier=0, allow_small_or_imprecise_dtypes=True),
               incs=[(gp, 1)])
        P.emit("gpsimd", lambda g: g.memset(ones1[:, :], 1.0), incs=[(gp, 1)])
        P.emit("gpsimd", lambda g: g.dma_start(out=biasb[:, :], in_=biasbc_d[:, :]), incs=[(dInit, 16)])
        P.emit("gpsimd", lambda g: g.dma_start(
            out=gw_sb[:, :].rearrange("p (k e) -> p k e", k=KC),
            in_=gwT_d[:, :].rearrange("(k p) e -> p k e", p=128)), incs=[(dInit, 16)])
        init_gp = now(gp)
        init_dW = now(dInit)
        P.emit("vector", lambda v: v.tensor_single_scalar(ident[:, :], ident[:, :], 0.0, ALU.is_equal),
               waits=[init_gp], incs=[(vec, 1)])
        ident_tok = now(vec)
        P.emit("vector", lambda v: v.tensor_copy(identb[:, :], ident[:, :]),
               waits=[ident_tok], incs=[(vec, 1)])
        identb_tok = now(vec)
        P.emit("gpsimd", lambda g: g.iota(iotaP[:, :], pattern=[[0, 1]], base=0,
                                          channel_multiplier=1,
                                          allow_small_or_imprecise_dtypes=True),
               incs=[(gp, 1)])
        P.emit("gpsimd", lambda g: g.iota(iotaW8[:, :], pattern=[[0, 1]], base=8 * 128,
                                          channel_multiplier=1), incs=[(gp, 1)])
        P.emit("gpsimd", lambda g: g.iota(thr8[:, :], pattern=[[256, 8]], base=0,
                                          channel_multiplier=0,
                                          allow_small_or_imprecise_dtypes=True),
               incs=[(gp, 1)])
        P.emit("gpsimd", lambda g: g.memset(zb16[:, :], 0.0), incs=[(gp, 1)])
        init_gp = now(gp)
        for zi in range((T + 128) // 128):
            P.emit("gpsimd", lambda g, zi=zi: g.dma_start(
                out=ynat_d[zi * 128:(zi + 1) * 128, :], in_=zb16[:, :]),
                   waits=[init_gp] if zi == 0 else [], incs=[(dRg, 16)])
        for e in range(NE):
            P.emit("gpsimd", lambda g, e=e: g.dma_start(
                out=wsu_d[e * 128:(e + 1) * 128, :].rearrange("p (k n) -> p k n", k=KC),
                in_=upT_d[e:e + 1, :, :].rearrange("o (k p) n -> p (o k) n", p=128)),
                   incs=[(dRg, 16)])
            P.emit("gpsimd", lambda g, e=e: g.dma_start(
                out=wsd_d[e * 128:(e + 1) * 128, :].rearrange("p (t n) -> p t n", t=HID // 128),
                in_=dnT_d[e:e + 1, :, :].rearrange("o (t p) n -> p (o t) n", p=128)),
                   incs=[(dRg, 16)])

        def engine_barrier():
            target = bar.v + 5
            dma_fence = {"sync": [now(q) for q in dXs + dYs + dPl + dRb +
                                  [dTt, dPinit, dB[0][2], dB[1][2]]],
                         "gpsimd": [now(q) for q in [dInit, dRg, dSc, dDbg,
                                    dB[0][0], dB[0][1], dB[1][0], dB[1][1]] +
                                    dYn + dWu + dWd + dGe + dR + dG],
                         "scalar": [now(q) for q in dM]}
            for e in ("tensor", "vector", "scalar", "gpsimd", "sync"):
                P.emit(e, lambda eng: eng.nop(), waits=dma_fence.get(e, []), incs=[(bar, 1)])
            for e in ("tensor", "vector", "scalar", "gpsimd", "sync"):
                P.emit(e, lambda eng: eng.nop(), waits=[(bar, target)])

        def emit_ln_tile(war, src_dram, ti, r):
            st = stt_[r]
            mu, var = st[:, 12:13], st[:, 13:14]
            rstd, nm = st[:, 14:15], st[:, 15:16]
            P.emit("sync", lambda sy: sy.dma_start(out=xr[r][:, :],
                                                   in_=src_dram[ti * 128:(ti + 1) * 128, :]),
                   waits=war.pop(("xr", r), []), incs=[(dXs[r], 16)])
            ld = now(dXs[r])
            stw = war.pop(("st", r), [])
            ybw = war.pop(("ybuf", r), [])
            P.emit("vector", lambda v: v.bn_stats(st[:, 0:6], xr[r][:, 0:512]),
                   waits=[ld] + stw, incs=[(vec, 1)])
            P.emit("vector", lambda v: v.bn_stats(st[:, 6:12], xr[r][:, 512:1024]),
                   waits=[now(vec)], incs=[(vec, 1)])
            P.emit("vector", lambda v: v.bn_aggr(st[:, 12:14], st[:, 0:12]),
                   waits=[now(vec)], incs=[(vec, 1)])
            agg_tok = now(vec)
            P.emit("scalar", lambda s: s.activation(rstd, var, AF.Sqrt, bias=f32(1e-5)),
                   waits=[agg_tok], incs=[(act, 1)])
            sqr_tok = now(act)
            P.emit("vector", lambda v: v.reciprocal(rstd, rstd), waits=[sqr_tok], incs=[(vec, 1)])
            rstd_tok = now(vec)
            P.emit("vector", lambda v: v.scalar_tensor_tensor(nm, mu, -1.0, rstd, ALU.mult, ALU.mult),
                   waits=[rstd_tok], incs=[(vec, 1)])
            nm_tok = now(vec)
            P.emit("scalar", lambda s: s.activation(ybuf[r][:, :], xr[r][:, :], AF.Identity,
                                                    bias=nm, scale=rstd),
                   waits=[rstd_tok, nm_tok] + ybw, incs=[(act, 1)])
            y_tok = now(act)
            war[("st", r)] = [y_tok]
            return {"ld": ld, "y": y_tok}

        def ffn_phase(src_dram, dst_dram, tag):
            NG = 15
            GS = 256
            NTSH = HID // 128
            with ExitStack() as ph:
                psb = lambda n_, sh, dt: ph.enter_context(nc.sbuf_tensor(n_ + tag, sh, dt))
                mmF = [psb(f"mmF{r}", [128, 128], F32) for r in range(2)]
                mT = [psb(f"mTr{r}", [128, 128], F32) for r in range(2)]
                sc_t = psb("sc_t", [128, 8], F32)
                bb_t = psb("bb_t", [128, 8], F32)
                t1_t = psb("t1_t", [128, 8], F32)
                oh_t = psb("oh_t", [128, 8], F32)
                v1_t = psb("v1_t", [128, 1], F32)
                i1_t = psb("i1_t", [128, 1], F32)
                yTf32 = psb("yTf32", [128, D], F32)
                fsc = [psb(f"fsc{i}", [128, 128], F32) for i in range(2)]
                flg8 = psb("flg8", [128, 8 * 128], F32)
                pos8 = psb("pos8", [128, 8 * 128], F32)
                ptrg = psb("ptrg", [128, 128], F32)
                scr = psb("scr", [128, 256], F32)
                gthr = psb("gthr", [128, 16], F32)
                ebsS = psb("ebsS", [128, 16], F32)
                widxF = psb("widxF", [128, 16], F32)
                egb = psb("egb", [128, 16], BF16)
                iotag = psb("iotag", [128, 16], I32)
                iotagF = psb("iotagF", [128, 16], F32)
                padb = psb("padb", [128, 64], F32)
                pmsh = psb("pmsh", [128, 32], F32)
                pm = [psb(f"pm{r}", [128, 32], F32) for r in range(2)]
                ptrI = [[psb(f"ptrI{hh}{r}", [128, 16], I32) for r in range(2)]
                        for hh in range(2)]
                widxI = [psb(f"widxI{r}", [128, 16], I32) for r in range(2)]
                prb = [psb(f"prb{s}", [128, 4], F32) for s in range(2)]
                idI = [psb(f"idI{s}", [128, 2], I32) for s in range(2)]
                wsu_sb = [psb(f"wsu{s}", [128, KC * HID], BF16) for s in range(2)]
                wsd_sb = [psb(f"wsd{s}", [128, NTSH * D], BF16) for s in range(2)]
                ge = [psb(f"ge{s}", [128, 2 * D], BF16) for s in range(2)]
                geT = [psb(f"geT{s}", [128, KC * GS], BF16) for s in range(2)]
                hh = [psb(f"hh{s}", [128, NTSH * GS], BF16) for s in range(2)]
                rb_out = [psb(f"rb{s}", [128, 2 * D], F32) for s in range(2)]
                bt = [[psb(f"bt{pq}{i}", [128, D], F32) for i in range(3)] for pq in range(2)]

                U = [PB[0][:, 0:GS], PB[1][:, 0:GS]]
                DD = [PB[2], PB[3]]
                TP = [PB[5][:, 0:128], PB[6][:, 0:128], PB[7][:, 128:256]]
                G8 = PB[7][:, 0:8]
                EB = PB[4][:, 32:48]
                TPm = PB[4][:, 0:128]

                war = {}
                mT_tok = {}
                ptrI_tok = {}
                widx_done = {}
                scat_done = [None]
                ynat_all = [None]
                iotF = {}
                pinit_tok_d = {}
                route_done = {}
                reorg_tok = (dRg, ((T + 128) // 128 + 2 * NE) * 16)

                def emit_iotas(half):
                    base_tok = half * HALF
                    P.emit("gpsimd", lambda g, base_tok=base_tok: g.iota(
                        iotag[:, :], pattern=[[128, 16]], base=base_tok,
                        channel_multiplier=1),
                           waits=war.pop("iotag", []), incs=[(gp, 1)])
                    P.emit("vector", lambda v: v.tensor_copy(iotagF[:, :], iotag[:, :]),
                           waits=[now(gp)] + war.pop("iotagF", []), incs=[(vec, 1)])
                    iotF[half] = now(vec)
                    if half == 0:
                        P.emit("gpsimd", lambda g: g.iota(
                            gthr[:, :], pattern=[[256, 16]], base=0, channel_multiplier=0,
                            allow_small_or_imprecise_dtypes=True), incs=[(gp, 1)])
                        P.emit("gpsimd", lambda g: g.memset(padb[:, :], 0.0), incs=[(gp, 1)])
                        P.emit("gpsimd", lambda g: g.memset(padb[:, 0:64:2], float(T)),
                               waits=[now(gp)], incs=[(gp, 1)])
                    return now(gp)

                def emit_rank_pinit(half, iob):
                    for r in range(2):
                        P.emit("sync", lambda sy, r=r: sy.dma_start(
                            out=pairs_d[r * 4096:(r + 1) * 4096, :].rearrange(
                                "(c p) f -> p c f", p=128),
                            in_=padb[:, :].rearrange("p (c f) -> p c f", f=2)),
                               waits=iob + war.pop(("pinit", r), []),
                               incs=[(dPinit, 16)])

                def emit_pinitsh(half):
                    P.emit("gpsimd", lambda g: g.memset(pmsh[:, :], 1.0),
                           waits=war.pop("pmsh", []), incs=[(gp, 1)])
                    P.emit("vector", lambda v: v.tensor_copy(pmsh[:, 0:32:2], iotagF[:, :]),
                           waits=[now(gp), iotF[half]], incs=[(vec, 1)])
                    P.emit("sync", lambda sy: sy.dma_start(
                        out=pairs_d[8192:8192 + HALF, :].rearrange("(c p) f -> p c f", p=128),
                        in_=pmsh[:, :].rearrange("p (c f) -> p c f", f=2)),
                           waits=[now(vec)] + war.pop("pinitsh", []), incs=[(dPinit, 16)])

                def emit_prepass_tile(half, i):
                    t0 = half * HTILES
                    r = i % 2
                    toks = emit_ln_tile(war, src_dram, t0 + i, r)
                    P.emit("gpsimd", lambda g, i=i, r=r, t0=t0: g.dma_start(
                        out=ynat_d[(t0 + i) * 128:(t0 + i + 1) * 128, :],
                        in_=ybuf[r][:, :]),
                           waits=[toks["y"]] + ([reorg_tok] if (half == 0 and i == 0) else []),
                           incs=[(dYn[r], 16)])
                    yn_tok = now(dYn[r])
                    evac_toks = []
                    ytfw = war.pop("yTf32", [])
                    for k in range(KC):
                        tp = (i * KC + k) % 2
                        P.emit("tensor", lambda t, k=k, r=r, tp=tp:
                               t.transpose(TP[tp], ybuf[r][:, k * 128:(k + 1) * 128], ident[:, :]),
                               waits=[toks["y"], ident_tok] + war.pop(("tp", tp), []),
                               incs=[(pe, 1)])
                        tp_tok = now(pe)
                        if k % 2 == 0:
                            P.emit("scalar", lambda s, k=k, tp=tp:
                                   s.copy(yTf32[:, k * 128:(k + 1) * 128], TP[tp]),
                                   waits=[tp_tok] + (ytfw if k == 0 else []),
                                   incs=[(act, 1)])
                            ev = now(act)
                        else:
                            P.emit("vector", lambda v, k=k, tp=tp:
                                   v.tensor_copy(yTf32[:, k * 128:(k + 1) * 128], TP[tp]),
                                   waits=[tp_tok] + (ytfw if k == 1 else []),
                                   incs=[(vec, 1)])
                            ev = now(vec)
                        evac_toks.append(ev)
                        war[("tp", tp)] = [ev]
                    war[("ybuf", r)] = [now(pe), yn_tok]
                    war[("xr", r)] = [toks["y"], now(vec)]
                    g8w = war.pop("g8", [])
                    for k in range(KC):
                        P.emit("tensor", lambda t, k=k:
                               t.matmul(G8, yTf32[:, k * 128:(k + 1) * 128],
                                        gw_sb[:, k * 8:(k + 1) * 8],
                                        start=(k == 0), stop=(k == KC - 1), skip_group_check=True),
                               waits=([evac_toks[KC - 2], evac_toks[KC - 1], init_dW] + g8w)
                                     if k == 0 else [],
                               incs=[(pe, 1)] if k == KC - 1 else [])
                    gate_tok = now(pe)
                    war["yTf32"] = [gate_tok]
                    P.emit("scalar", lambda s: s.activation(sc_t[:, :], G8, AF.Sigmoid),
                           waits=[gate_tok], incs=[(act, 1)])
                    sig_tok = now(act)
                    war["g8"] = [sig_tok]
                    war["g8read"] = [sig_tok]
                    rw = war.pop("route", [])
                    P.emit("vector", lambda v: v.tensor_add(bb_t[:, :], sc_t[:, :], biasb[:, :]),
                           waits=[sig_tok, init_dW] + rw, incs=[(vec, 1)])
                    P.emit("vector", lambda v: v.reduce_max(v1_t[:, :], bb_t[:, :], axis=AX.X),
                           waits=[now(vec)], incs=[(vec, 1)])
                    P.emit("vector", lambda v: v.tensor_scalar(t1_t[:, :], bb_t[:, :], v1_t[:, 0:1],
                                                               -100.0, ALU.is_ge, ALU.mult),
                           waits=[now(vec)], incs=[(vec, 1)])
                    P.emit("vector", lambda v: v.tensor_tensor(t1_t[:, :], t1_t[:, :], iota8[:, :],
                                                               ALU.add),
                           waits=[init_gp, now(vec)], incs=[(vec, 1)])
                    P.emit("vector", lambda v: v.tensor_reduce(i1_t[:, :], t1_t[:, :], axis=AX.X,
                                                               op=ALU.min),
                           waits=[now(vec)], incs=[(vec, 1)])
                    P.emit("vector", lambda v: v.tensor_single_scalar(oh_t[:, :], t1_t[:, :],
                                                                      i1_t[:, 0:1], ALU.is_equal),
                           waits=[now(vec)], incs=[(vec, 1)])
                    oh1_tok = now(vec)
                    P.emit("vector", lambda v: v.scalar_tensor_tensor(bb_t[:, :], oh_t[:, :], -1e30,
                                                                      bb_t[:, :], ALU.mult, ALU.add),
                           waits=[now(vec)], incs=[(vec, 1)])
                    P.emit("vector", lambda v: v.reduce_max(v1_t[:, :], bb_t[:, :], axis=AX.X),
                           waits=[now(vec)], incs=[(vec, 1)])
                    P.emit("vector", lambda v: v.tensor_scalar(t1_t[:, :], bb_t[:, :], v1_t[:, 0:1],
                                                               -100.0, ALU.is_ge, ALU.mult),
                           waits=[now(vec)], incs=[(vec, 1)])
                    P.emit("vector", lambda v: v.tensor_tensor(t1_t[:, :], t1_t[:, :], iota8[:, :],
                                                               ALU.add),
                           waits=[now(vec)], incs=[(vec, 1)])
                    P.emit("vector", lambda v: v.tensor_reduce(i1_t[:, :], t1_t[:, :], axis=AX.X,
                                                               op=ALU.min),
                           waits=[now(vec)], incs=[(vec, 1)])
                    P.emit("vector", lambda v: v.tensor_single_scalar(t1_t[:, :], t1_t[:, :],
                                                                      i1_t[:, 0:1], ALU.is_equal),
                           waits=[now(vec)], incs=[(vec, 1)])
                    P.emit("vector", lambda v, i=i: v.tensor_mul(mmF[0][:, i:i + 113:16],
                                                                 sc_t[:, :], oh_t[:, :]),
                           waits=[now(vec), oh1_tok] + (war.pop(("mmFw", 0), []) if i == 0 else []),
                           incs=[(vec, 1)])
                    P.emit("vector", lambda v, i=i: v.tensor_mul(mmF[1][:, i:i + 113:16],
                                                                 sc_t[:, :], t1_t[:, :]),
                           waits=[now(vec)] + (war.pop(("mmFw", 1), []) if i == 0 else []),
                           incs=[(vec, 1)])
                    m_tok = now(vec)
                    war["route"] = [m_tok]
                    if i == HTILES - 1:
                        route_done[half] = m_tok
                        P.emit("tensor", lambda t: t.transpose(TPm, mmF[0][:, :], ident[:, :]),
                               waits=[m_tok] + war.pop("tpm", []) + war.pop("pb4", []),
                               incs=[(pe, 1)])
                        tok1 = now(pe)
                        P.emit("scalar", lambda s: s.copy(mT[0][:, :], TPm),
                               waits=[tok1] + war.pop(("mTw", 0), []), incs=[(act, 1)])
                        mT_tok[0] = now(act)
                        P.emit("tensor", lambda t: t.transpose(TPm, mmF[1][:, :], ident[:, :]),
                               waits=[m_tok, mT_tok[0]], incs=[(pe, 1)])
                        tok2 = now(pe)
                        P.emit("scalar", lambda s: s.copy(mT[1][:, :], TPm),
                               waits=[tok2] + war.pop(("mTw", 1), []), incs=[(act, 1)])
                        mT_tok[1] = now(act)
                        war["tpm"] = [mT_tok[1]]
                        war["pb4"] = [mT_tok[1]]
                        war[("mmFw", 0)] = [tok1]
                        war[("mmFw", 1)] = [tok2]

                def emit_scan_main(r, half):
                    ro = 128 * r
                    m_tok = route_done[half]
                    for e in range(8):
                        sl = e % 2
                        P.emit("scalar", lambda s, e=e, sl=sl, r=r: s.dma_start(
                            out=fsc[sl][0:16, 0:128], in_=mT[r][16 * e:16 * e + 16, :]),
                               waits=[mT_tok[r]] + war.pop(("fsc", sl), []),
                               incs=[(dM[sl], 16)])
                        fl = now(dM[sl])
                        P.emit("vector", lambda v, e=e, sl=sl: v.tensor_single_scalar(
                            flg8[0:16, e * 128:(e + 1) * 128], fsc[sl][0:16, 0:128],
                            0.0, ALU.is_gt),
                               waits=[fl], incs=[(vec, 1)])
                        war[("fsc", sl)] = [now(vec)]
                        P.emit("vector", lambda v, e=e: v.tensor_tensor_scan(
                            pos8[0:16, e * 128:(e + 1) * 128],
                            flg8[0:16, e * 128:(e + 1) * 128],
                            flg8[0:16, e * 128:(e + 1) * 128], 0.0, ALU.add, ALU.bypass),
                               waits=[now(vec)] + war.pop(("pos8", e), []), incs=[(vec, 1)])
                        sc_tok = now(vec)
                        P.emit("sync", lambda sy, e=e, r=r: sy.dma_start(
                            out=tt_hbm[r, e * 16:(e + 1) * 16],
                            in_=pos8[0:16, e * 128 + 127:e * 128 + 128]),
                               waits=[sc_tok], incs=[(dTt, 16)])
                    tt_done = now(dTt)
                    for e in range(8):
                        war[("pos8", e)] = [tt_done]
                    P.emit("sync", lambda sy, r=r, ro=ro: sy.dma_start(
                        out=scr[0:8, ro + 0:ro + 16],
                        in_=tt_hbm[r, :].rearrange("(e t) -> e t", e=8)),
                           waits=[tt_done] + war.pop(("scrA", r), []), incs=[(dTt, 16)])
                    t8l = now(dTt)
                    P.emit("vector", lambda v, ro=ro: v.tensor_reduce(
                        scr[0:8, ro + 48:ro + 49], scr[0:8, ro + 0:ro + 16],
                        axis=AX.X, op=ALU.add),
                           waits=[t8l], incs=[(vec, 1)])
                    P.emit("vector", lambda v, ro=ro: v.tensor_single_scalar(
                        scr[0:8, ro + 50:ro + 58], thr8[0:8, :],
                        scr[0:8, ro + 48:ro + 49], ALU.is_lt),
                           waits=[now(vec), init_gp], incs=[(vec, 1)])
                    P.emit("vector", lambda v, ro=ro: v.tensor_reduce(
                        scr[0:8, ro + 49:ro + 50], scr[0:8, ro + 50:ro + 58],
                        axis=AX.X, op=ALU.add),
                           waits=[now(vec)], incs=[(vec, 1)])
                    P.emit("vector", lambda v, ro=ro: v.tensor_single_scalar(
                        scr[0:8, ro + 49:ro + 50], scr[0:8, ro + 49:ro + 50],
                        256.0, ALU.mult),
                           waits=[now(vec)], incs=[(vec, 1)])
                    pd_tok = now(vec)
                    P.emit("sync", lambda sy, r=r, ro=ro: sy.dma_start(
                        out=po_hbm[r, 0:8], in_=scr[0:8, ro + 49:ro + 50]),
                           waits=[pd_tok], incs=[(dTt, 16)])
                    P.emit("sync", lambda sy, r=r, ro=ro: sy.dma_start(
                        out=scr[0:1, ro + 58:ro + 66], in_=po_hbm[r, 0:8]),
                           waits=[now(dTt)], incs=[(dTt, 16)])
                    prw = now(dTt)
                    P.emit("vector", lambda v, ro=ro: v.tensor_tensor_scan(
                        scr[0:1, ro + 66:ro + 74], scr[0:1, ro + 58:ro + 66],
                        scr[0:1, ro + 58:ro + 66], 0.0, ALU.add, ALU.bypass),
                           waits=[prw], incs=[(vec, 1)])
                    P.emit("vector", lambda v, ro=ro: v.tensor_tensor(
                        scr[0:1, ro + 74:ro + 82], scr[0:1, ro + 66:ro + 74],
                        scr[0:1, ro + 58:ro + 66], ALU.subtract),
                           waits=[now(vec)], incs=[(vec, 1)])
                    pofs_tok = now(vec)
                    P.emit("sync", lambda sy, r=r, ro=ro: sy.dma_start(
                        out=po_hbm[r, 8:16], in_=scr[0:1, ro + 74:ro + 82]),
                           waits=[pofs_tok], incs=[(dTt, 16)])
                    P.emit("sync", lambda sy, r=r, ro=ro: sy.dma_start(
                        out=scr[0:8, ro + 82:ro + 83], in_=po_hbm[r, 8:16],
                        allow_slow_non_contiguous=True),
                           waits=[now(dTt)], incs=[(dTt, 16)])
                    pofsc = now(dTt)
                    P.emit("vector", lambda v, ro=ro: v.tensor_tensor_scan(
                        scr[0:8, ro + 16:ro + 32], scr[0:8, ro + 0:ro + 16],
                        scr[0:8, ro + 0:ro + 16], 0.0, ALU.add, ALU.bypass),
                           waits=[t8l], incs=[(vec, 1)])
                    P.emit("vector", lambda v, ro=ro: v.tensor_tensor(
                        scr[0:8, ro + 32:ro + 48], scr[0:8, ro + 16:ro + 32],
                        scr[0:8, ro + 0:ro + 16], ALU.subtract),
                           waits=[now(vec)], incs=[(vec, 1)])
                    P.emit("vector", lambda v, ro=ro, r=r: v.tensor_scalar(
                        scr[0:8, ro + 32:ro + 48], scr[0:8, ro + 32:ro + 48],
                        scr[0:8, ro + 82:ro + 83], float(4096 * r), ALU.add, ALU.add),
                           waits=[now(vec), pofsc], incs=[(vec, 1)])
                    off_tok = now(vec)
                    P.emit("tensor", lambda t, ro=ro: t.transpose(
                        PB[4][0:16, 0:8], scr[0:8, ro + 32:ro + 48], ident[0:8, 0:8]),
                           waits=[off_tok, ident_tok] + war.pop("pb4", []),
                           incs=[(pe, 1)])
                    P.emit("scalar", lambda sc_, ro=ro: sc_.copy(
                        scr[0:16, ro + 83:ro + 91], PB[4][0:16, 0:8]),
                           waits=[now(pe)], incs=[(act, 1)])
                    offT_tok = now(act)
                    war["pb4"] = [offT_tok]
                    P.emit("gpsimd", lambda g: g.memset(ptrg[0:16, 0:128], 0.0),
                           waits=war.pop("ptrg", []), incs=[(gp, 1)])
                    ptrg_tok = now(gp)
                    fw = war.pop(("fsc", 0), [])
                    for e in range(8):
                        P.emit("vector", lambda v, e=e, ro=ro: v.tensor_scalar(
                            fsc[0][0:16, 0:128], pos8[0:16, e * 128:(e + 1) * 128],
                            scr[0:16, ro + 83 + e:ro + 84 + e], -1.0, ALU.add, ALU.add),
                               waits=[offT_tok] + ([now(dM[0])] + fw if e == 0
                                                   else [now(vec)]),
                               incs=[(vec, 1)])
                        P.emit("vector", lambda v, e=e: v.tensor_tensor(
                            fsc[0][0:16, 0:128], fsc[0][0:16, 0:128],
                            flg8[0:16, e * 128:(e + 1) * 128], ALU.mult),
                               waits=[now(vec)], incs=[(vec, 1)])
                        P.emit("vector", lambda v, e=e: v.tensor_tensor(
                            ptrg[0:16, 0:128], ptrg[0:16, 0:128],
                            fsc[0][0:16, 0:128], ALU.add),
                               waits=[now(vec)] + ([ptrg_tok] if e == 0 else []),
                               incs=[(vec, 1)])
                    ptr_done = now(vec)
                    war[("fsc", 0)] = [ptr_done]
                    P.emit("tensor", lambda t: t.transpose(
                        PB[4][:, 16:32], ptrg[0:16, 0:128], ident[0:16, 0:16]),
                           waits=[ptr_done, ident_tok] + war.pop("pb4", []),
                           incs=[(pe, 1)])
                    ptp = now(pe)
                    war["ptrg"] = [ptp]
                    P.emit("vector", lambda v, r=r, half=half: v.tensor_copy(
                        ptrI[half][r][:, :], PB[4][:, 16:32]),
                           waits=[ptp] + war.pop(("ptrIw", half, r), []), incs=[(vec, 1)])
                    ptrI_tok[(half, r)] = now(vec)
                    war["pb4"] = [ptrI_tok[(half, r)]]
                    pmw = war.pop(("pmw", r), [])
                    P.emit("vector", lambda v, r=r: v.tensor_copy(pm[r][:, 0:32:2], iotagF[:, :]),
                           waits=[iotF[half]] + pmw, incs=[(vec, 1)])
                    for i in range(16):
                        P.emit("vector", lambda v, i=i, r=r: v.tensor_reduce(
                            pm[r][:, 2 * i + 1:2 * i + 2], mmF[r][:, i:i + 113:16],
                            axis=AX.X, op=ALU.add),
                               waits=[m_tok] if i == 0 else [now(vec)], incs=[(vec, 1)])
                    war[("pm", r)] = [now(vec)]
                    war[("scrA", r)] = [ptr_done]

                def emit_scan_post(half):
                    for r in range(2):
                        ro = 128 * r
                        ebw = war.pop("egrp", [])
                        for e in range(8):
                            dstc = (91, 107) if e == 0 else (107, 123)
                            P.emit("vector", lambda v, e=e, ro=ro, dstc=dstc: v.tensor_single_scalar(
                                scr[0:1, ro + dstc[0]:ro + dstc[1]], gthr[0:1, :],
                                scr[0:1, ro + 66 + e:ro + 67 + e], ALU.is_ge),
                                   waits=([init_gp, ptrI_tok[(half, r)]] + ebw)
                                         if e == 0 else [now(vec)],
                                   incs=[(vec, 1)])
                            if e > 0:
                                P.emit("vector", lambda v, ro=ro: v.tensor_tensor(
                                    scr[0:1, ro + 91:ro + 107], scr[0:1, ro + 91:ro + 107],
                                    scr[0:1, ro + 107:ro + 123], ALU.add),
                                       waits=[now(vec)], incs=[(vec, 1)])
                        P.emit("vector", lambda v, ro=ro: v.tensor_single_scalar(
                            scr[0:1, ro + 91:ro + 107], scr[0:1, ro + 91:ro + 107],
                            7.0, ALU.min),
                               waits=[now(vec)], incs=[(vec, 1)])
                        P.emit("vector", lambda v, ro=ro: v.tensor_copy(
                            egb[0:1, :], scr[0:1, ro + 91:ro + 107]),
                               waits=[now(vec)] + war.pop("egb", []), incs=[(vec, 1)])
                        egb_tok = now(vec)
                        P.emit("tensor", lambda t: t.matmul(
                            EB, ones1[0:1, :], egb[0:1, :], start=True, stop=True,
                            skip_group_check=True),
                               waits=[egb_tok, init_gp] + war.pop("pb4", []),
                               incs=[(pe, 1)])
                        eb_tok = now(pe)
                        P.emit("scalar", lambda s: s.copy(ebsS[:, :], EB),
                               waits=[eb_tok] + war.pop("ebs", []), incs=[(act, 1)])
                        war["pb4"] = [now(act)]
                        P.emit("vector", lambda v: v.tensor_single_scalar(
                            widxF[:, :], ebsS[:, :], 128.0, ALU.mult),
                               waits=[now(act)] + war.pop("widxF", []), incs=[(vec, 1)])
                        war["ebs"] = [now(vec)]
                        P.emit("vector", lambda v: v.tensor_single_scalar(
                            widxF[:, :], widxF[:, :], iotaP[:, 0:1], ALU.add),
                               waits=[now(vec), init_gp], incs=[(vec, 1)])
                        P.emit("vector", lambda v, r=r: v.tensor_copy(widxI[r][:, :], widxF[:, :]),
                               waits=[now(vec)] + war.pop(("widxw", r), []), incs=[(vec, 1)])
                        widx_done[r] = now(vec)
                        war["widxF"] = [widx_done[r]]
                        war["egb"] = [eb_tok]
                    for r in range(2):
                        for i in range(16):
                            P.emit("gpsimd", lambda g, r=r, i=i, half=half: g.indirect_dma_start(
                                out=pairs_d[:, :],
                                out_offset=IndirectOffsetOnAxis(ptrI[half][r][:, i:i + 1], 0),
                                in_=pm[r][:, 2 * i:2 * i + 2], in_offset=None,
                                oob_is_err=False),
                                   waits=([ptrI_tok[(half, r)], pinit_tok_d[half]] + war[("pm", r)])
                                         if i == 0 else [],
                                   incs=[(dSc, 16)])
                    scat_done[0] = now(dSc)
                    for r in range(2):
                        war[("pmw", r)] = [scat_done[0]]

                def emit_group_dmas(kind, r, g, gi, half):
                    s = gi % 2
                    if kind == "s":
                        base = 8192 + g * GS
                        pwait = [pinit_tok_d[half]]
                    else:
                        base = r * 4096 + g * GS
                        pwait = [scat_done[0]]
                    P.emit("sync", lambda sy, base=base, s=s: sy.dma_start(
                        out=prb[s][:, :].rearrange("p (c f) -> p c f", f=2),
                        in_=pairs_d[base:base + GS, :].rearrange("(c p) f -> p c f", p=128)),
                           waits=war.pop(("prb", s), []) + pwait, incs=[(dPl[s], 16)])
                    pl = now(dPl[s])
                    P.emit("vector", lambda v, s=s: v.tensor_copy(idI[s][:, :], prb[s][:, 0:4:2]),
                           waits=[pl] + war.pop(("idI", s), []), incs=[(vec, 1)])
                    id_tok = now(vec)
                    gw_ = war.pop(("ge", s), [])
                    for c in range(2):
                        P.emit("gpsimd", lambda g_, c=c, s=s: g_.indirect_dma_start(
                            out=ge[s][:, c * D:(c + 1) * D], out_offset=None,
                            in_=ynat_d[:, :],
                            in_offset=IndirectOffsetOnAxis(idI[s][:, c:c + 1], 0),
                            oob_is_err=False),
                               waits=([id_tok] + ynat_all[0] + gw_) if c == 0 else [],
                               incs=[(dGe[s], 16)])
                    ge_tok = now(dGe[s])
                    if kind == "s":
                        wio = lambda: iotaW8[:, 0:1]
                        wtok = [init_gp]
                    else:
                        wio = lambda r=r, g=g: widxI[r][:, g:g + 1]
                        wtok = [widx_done[r]]
                    P.emit("gpsimd", lambda g_, s=s, wio=wio: g_.indirect_dma_start(
                        out=wsu_sb[s][:, :], out_offset=None, in_=wsu_d[:, :],
                        in_offset=IndirectOffsetOnAxis(wio(), 0), oob_is_err=False),
                           waits=[reorg_tok] + wtok + war.pop(("wsu", s), []),
                           incs=[(dWu[s], 16)])
                    P.emit("gpsimd", lambda g_, s=s, wio=wio: g_.indirect_dma_start(
                        out=wsd_sb[s][:, :], out_offset=None, in_=wsd_d[:, :],
                        in_offset=IndirectOffsetOnAxis(wio(), 0), oob_is_err=False),
                           waits=war.pop(("wsd", s), []),
                           incs=[(dWd[s], 16)])
                    return dict(s=s, pl=pl, idt=id_tok, ge=ge_tok,
                                wu=now(dWu[s]), wd=now(dWd[s]))

                def emit_group_compute(dm, kind, r, g, half):
                    s = dm["s"]
                    sc_toks = []
                    for c in range(2):
                        P.emit("vector", lambda v, c=c, s=s: v.tensor_single_scalar(
                            ge[s][:, c * D:(c + 1) * D], ge[s][:, c * D:(c + 1) * D],
                            prb[s][:, 2 * c + 1:2 * c + 2], ALU.mult),
                               waits=[dm["ge"], dm["pl"]] if c == 0 else [now(vec)],
                               incs=[(vec, 1)])
                        sc_toks.append(now(vec))
                    gtw = war.pop(("geT", s), [])
                    last_act, last_vec = None, None
                    for c in range(2):
                        for k in range(KC):
                            tp = (c * KC + k) % 3
                            P.emit("tensor", lambda t, c=c, k=k, tp=tp, s=s:
                                   t.transpose(TP[tp].bitcast(BF16)[:, 0:128],
                                               ge[s][:, c * D + k * 128:c * D + (k + 1) * 128],
                                               identb[:, :]),
                                   waits=[sc_toks[c], identb_tok] + war.pop(("tp", tp), []) +
                                         (war.pop("g8read", []) if tp == 2 else []),
                                   incs=[(pe, 1)])
                            tt_ = now(pe)
                            if (c * KC + k) % 2 == 0:
                                P.emit("scalar", lambda sc_, k=k, c=c, tp=tp, s=s:
                                       sc_.copy(geT[s][:, k * GS + c * 128:k * GS + (c + 1) * 128],
                                                TP[tp].bitcast(BF16)[:, 0:128]),
                                       waits=[tt_] + (gtw if (c == 0 and k == 0) else []),
                                       incs=[(act, 1)])
                                last_act = now(act)
                                war[("tp", tp)] = [last_act]
                            else:
                                P.emit("vector", lambda v, k=k, c=c, tp=tp, s=s:
                                       v.tensor_copy(geT[s][:, k * GS + c * 128:k * GS + (c + 1) * 128],
                                                     TP[tp].bitcast(BF16)[:, 0:128]),
                                       waits=[tt_] + (gtw if (c == 0 and k == 1) else []),
                                       incs=[(vec, 1)])
                                last_vec = now(vec)
                                war[("tp", tp)] = [last_vec]
                    war[("ge", s)] = [now(pe)]
                    geT_rdy = [last_act, last_vec]
                    sq_toks = []
                    u_tok = None
                    for m in range(NTSH):
                        ub = m % 2
                        uw = war.pop(("U", ub), [])
                        for k in range(KC):
                            P.emit("tensor", lambda t, m=m, k=k, s=s, ub=ub:
                                   t.matmul(U[ub],
                                            wsu_sb[s][:, k * HID + m * 128:k * HID + (m + 1) * 128],
                                            geT[s][:, k * GS:(k + 1) * GS],
                                            start=(k == 0), stop=(k == KC - 1),
                                            skip_group_check=True),
                                   waits=([dm["wu"]] + geT_rdy + uw) if k == 0 else [],
                                   incs=[(pe, 1)] if k == KC - 1 else [])
                        u_tok = now(pe)
                        P.emit("scalar", lambda sc_, m=m, s=s, ub=ub:
                               sc_.activation(hh[s][:, m * GS:(m + 1) * GS], U[ub], AF.Relu),
                               waits=[u_tok] + (war.pop(("hh", s), []) if m == 0 else []),
                               incs=[(act, 1)])
                        relu_tok = now(act)
                        war[("U", ub)] = [relu_tok]
                        P.emit("vector", lambda v, m=m, s=s: v.tensor_mul(
                            hh[s][:, m * GS:(m + 1) * GS], hh[s][:, m * GS:(m + 1) * GS],
                            hh[s][:, m * GS:(m + 1) * GS]),
                               waits=[relu_tok], incs=[(vec, 1)])
                        sq_toks.append(now(vec))
                    war[("wsu", s)] = [u_tok]
                    wrb = war.pop(("rbw", s), [])
                    ev4 = []
                    d_tok = None
                    for combo in range(4):
                        tt_i, cc = combo // 2, combo % 2
                        db = combo % 2
                        dw_ = war.pop(("DD", db), [])
                        for m in range(NTSH):
                            P.emit("tensor", lambda t, m=m, tt_i=tt_i, cc=cc, s=s, db=db:
                                   t.matmul(DD[db],
                                            hh[s][:, m * GS + tt_i * 128:m * GS + tt_i * 128 + 128],
                                            wsd_sb[s][:, m * D + cc * 512:m * D + (cc + 1) * 512],
                                            start=(m == 0), stop=(m == NTSH - 1),
                                            skip_group_check=True),
                                   waits=[sq_toks[m]] + ([dm["wd"]] + dw_ if m == 0 else []),
                                   incs=[(pe, 1)] if m == NTSH - 1 else [])
                        d_tok = now(pe)
                        P.emit("vector", lambda v, tt_i=tt_i, cc=cc, s=s, db=db: v.tensor_copy(
                            rb_out[s][:, tt_i * D + cc * 512:tt_i * D + (cc + 1) * 512], DD[db]),
                               waits=[d_tok] + (wrb if combo == 0 else []), incs=[(vec, 1)])
                        war[("DD", db)] = [now(vec)]
                        ev4.append(now(vec))
                    war[("wsd", s)] = [d_tok]
                    war[("hh", s)] = [d_tok]
                    if kind == "s":
                        dstb, rowbase = rbs_d, half * HALF + g * GS
                    else:
                        dstb, rowbase = rbr_d, r * 4096 + g * GS
                    for tt_i in range(2):
                        P.emit("sync", lambda sy, dstb=dstb, rowbase=rowbase, tt_i=tt_i, s=s:
                               sy.dma_start(
                                   out=dstb[rowbase + tt_i * 128:rowbase + (tt_i + 1) * 128, :],
                                   in_=rb_out[s][:, tt_i * D:(tt_i + 1) * D]),
                               waits=[ev4[tt_i * 2], ev4[tt_i * 2 + 1]] +
                                     (war.pop("rblistr", [])
                                      if (tt_i == 0 and kind != "s") else []),
                               incs=[(dRb[s], 16)])
                    war[("rbw", s)] = [now(dRb[s])]
                    war[("idI", s)] = [now(dGe[s])]
                    war[("prb", s)] = [sc_toks[1], dm["idt"]]

                pend = [None]

                def do_group(kind, r, g, gi, half):
                    dm = emit_group_dmas(kind, r, g, gi, half)
                    if pend[0] is not None:
                        emit_group_compute(*pend[0])
                    pend[0] = (dm, kind, r, g, half)

                def emit_residual_tile(half, i, rb_all):
                    t0 = half * HTILES
                    r = i % 2
                    P.emit("sync", lambda sy, i=i, r=r, t0=t0: sy.dma_start(
                        out=xr[r][:, :], in_=src_dram[(t0 + i) * 128:(t0 + i + 1) * 128, :]),
                           waits=war.pop(("xr", r), []), incs=[(dXs[r], 16)])
                    xl = now(dXs[r])
                    for bi in range(2):
                        P.emit("gpsimd", lambda g_, i=i, bi=bi, r=r, half=half: g_.indirect_dma_start(
                            out=bt[r][bi][:, :], out_offset=None,
                            in_=rbr_d[:, :],
                            in_offset=IndirectOffsetOnAxis(ptrI[half][bi][:, i:i + 1], 0),
                            oob_is_err=False),
                               waits=rb_all + [ptrI_tok[(half, bi)]] + war.pop(("bt", r, bi), []),
                               incs=[(dB[r][bi], 16)])
                    P.emit("sync", lambda sy, i=i, r=r, t0=t0: sy.dma_start(
                        out=bt[r][2][:, :], in_=rbs_d[(t0 + i) * 128:(t0 + i + 1) * 128, :]),
                           waits=rb_all + war.pop(("bt", r, 2), []), incs=[(dB[r][2], 16)])
                    b0t, b1t, b2t = now(dB[r][0]), now(dB[r][1]), now(dB[r][2])
                    P.emit("vector", lambda v, r=r: v.tensor_tensor(
                        bt[r][0][:, :], bt[r][0][:, :], bt[r][1][:, :], ALU.add),
                           waits=[b0t, b1t], incs=[(vec, 1)])
                    P.emit("vector", lambda v, r=r: v.tensor_tensor(
                        bt[r][0][:, :], bt[r][0][:, :], bt[r][2][:, :], ALU.add),
                           waits=[now(vec), b2t], incs=[(vec, 1)])
                    s2 = now(vec)
                    war[("bt", r, 1)] = [s2]
                    war[("bt", r, 2)] = [s2]
                    ybw = war.pop(("ybuf", r), [])
                    P.emit("vector", lambda v, r=r: v.scalar_tensor_tensor(
                        ybuf[r][:, :], bt[r][0][:, :], 0.5, xr[r][:, :], ALU.mult, ALU.add),
                           waits=[xl, s2] + ybw, incs=[(vec, 1)])
                    st_ = now(vec)
                    war[("xr", r)] = [st_]
                    war[("bt", r, 0)] = [st_]
                    P.emit("sync", lambda sy, i=i, r=r, t0=t0: sy.dma_start(
                        out=dst_dram[(t0 + i) * 128:(t0 + i + 1) * 128, :], in_=ybuf[r][:, :]),
                           waits=[st_], incs=[(dYs[r], 16)])
                    war[("ybuf", r)] = [now(dYs[r])]

                routed = [(r, g) for r in range(2) for g in range(NG)]
                iob0 = emit_iotas(0)
                emit_rank_pinit(0, [iob0])
                emit_pinitsh(0)
                pinit_tok_d[0] = now(dPinit)
                for i in range(HTILES):
                    emit_prepass_tile(0, i)
                ynat_all[0] = [now(dYn[0]), now(dYn[1])]
                gi = 0
                for g in (0, 1):
                    do_group("s", 0, g, gi, 0); gi += 1
                emit_scan_main(0, 0)
                for g in (2, 3):
                    do_group("s", 0, g, gi, 0); gi += 1
                emit_scan_main(1, 0)
                for g in range(4, 8):
                    do_group("s", 0, g, gi, 0); gi += 1
                emit_scan_post(0)
                war[("mTw", 0)] = [now(dM[0]), now(dM[1])]
                war[("mTw", 1)] = [now(dM[0]), now(dM[1])]
                tile_next = 0
                for idx, (r, g) in enumerate(routed):
                    do_group("r", r, g, gi, 0); gi += 1
                    while tile_next < HTILES and tile_next * (2 * NG - 2) <= idx * HTILES:
                        emit_prepass_tile(1, tile_next)
                        tile_next += 1
                while tile_next < HTILES:
                    emit_prepass_tile(1, tile_next)
                    tile_next += 1
                war[("widxw", 0)] = [now(dWu[0]), now(dWu[1]), now(dWd[0]), now(dWd[1])]
                war[("widxw", 1)] = [now(dWu[0]), now(dWu[1]), now(dWd[0]), now(dWd[1])]
                ynat_all[0] = [now(dYn[0]), now(dYn[1])]
                emit_group_compute(*pend[0])
                pend[0] = None
                rb_all0 = [now(dRb[0]), now(dRb[1])]
                iob1 = emit_iotas(1)
                war["pinitsh"] = [now(dPl[0]), now(dPl[1])]
                emit_pinitsh(1)
                war[("pinit", 0)] = [now(dPl[0]), now(dPl[1]), scat_done[0]]
                war[("pinit", 1)] = [now(dPl[0]), now(dPl[1]), scat_done[0]]
                emit_rank_pinit(1, [iob1])
                pinit_tok_d[1] = now(dPinit)
                for g in (0, 1):
                    do_group("s", 0, g, gi, 1); gi += 1
                emit_scan_main(0, 1)
                for g in (2, 3):
                    do_group("s", 0, g, gi, 1); gi += 1
                emit_scan_main(1, 1)
                for g in range(4, 8):
                    do_group("s", 0, g, gi, 1); gi += 1
                emit_scan_post(1)
                for i in range(HTILES):
                    emit_residual_tile(0, i, rb_all0)
                war["rblistr"] = [now(dB[0][0]), now(dB[0][1]), now(dB[0][2]),
                                  now(dB[1][0]), now(dB[1][1]), now(dB[1][2])]
                for (r, g) in routed:
                    do_group("r", r, g, gi, 1); gi += 1
                emit_group_compute(*pend[0])
                pend[0] = None
                rb_all1 = [now(dRb[0]), now(dRb[1])]
                for i in range(HTILES):
                    emit_residual_tile(1, i, rb_all1)
            engine_barrier()
        def attn_phase(src_dram, dst_dram):
            with ExitStack() as ph:
                psb = lambda n_, sh, dt: ph.enter_context(nc.sbuf_tensor(n_ + "_at", sh, dt))
                iota512 = psb("iota512", [128, 512], F32)
                wqkv_sb = psb("wqkv", [128, KC * 192], BF16)
                wout_sb = psb("wout", [64, D], BF16)
                zT = [psb(f"zT{i}", [128, KC * CHUNK], BF16) for i in range(2)]
                qkT = psb("qkT", [128, NB * SEQ], BF16)
                v_sb = psb("v_sb", [128, NTILES * 64], BF16)
                vI = psb("vI", [128, 16 * 128], BF16)
                qc_r = psb("qc_r", [1, 2 * SEQ], BF16)
                k_sb = psb("k_sb", [128, NTILES * 64], F32)
                Esb = [psb(f"Esb{i}", [128, 512], BF16) for i in range(8)]
                masks = psb("masks", [128, 4 * 512], BF16)
                ndbuf = psb("ndbuf", [128, SEQ], F32)
                rden = psb("rden", [64, SEQ], F32)
                oT = psb("oT", [64, SEQ], BF16)
                stg = [psb(f"stg{i}", [2, 512], F32) for i in range(4)]

                TPz = [PB[5][:, 0:128], PB[6][:, 0:128]]
                QK = PB[7]
                Vp = PB[4][:, 0:64]
                BQ = [PB[0], PB[1]]
                Kp = PB[2][:, 0:64]
                ND = [PB[5][0:2, :], PB[6][0:2, :]]
                OP = [PB[4], PB[7]]

                war = {}
                P.emit("gpsimd", lambda g: g.dma_start(
                    out=wqkv_sb[:, :].rearrange("p (k f) -> p k f", k=KC),
                    in_=wqkvT_d[:, :].rearrange("(k p) f -> p k f", p=128)), incs=[(dInit, 16)])
                P.emit("gpsimd", lambda g: g.dma_start(out=wout_sb[:, :], in_=woutT_d[:, :]),
                       incs=[(dInit, 16)])
                wqkv_tok = now(dInit)
                wout_tok = now(dInit)
                P.emit("gpsimd", lambda g: g.iota(iota512[:, :], pattern=[[1, 512]], base=0,
                                                  channel_multiplier=-1,
                                                  allow_small_or_imprecise_dtypes=True),
                       incs=[(gp, 1)])
                io5_tok = now(gp)
                for m in range(4):
                    P.emit("vector", lambda v, m=m: v.tensor_single_scalar(
                        masks[:, m * 512:(m + 1) * 512], iota512[:, :], float(128.0 * m), ALU.is_ge),
                           waits=[io5_tok], incs=[(vec, 1)])
                mask_tok = now(vec)

                for ch8 in range(8):
                    zr = ch8 % 2
                    last_evac = None
                    for j in range(4):
                        i = ch8 * 4 + j
                        r = i % 2
                        toks = emit_ln_tile(war, src_dram, i, r)
                        for k in range(KC):
                            tp = (i * KC + k) % 2
                            P.emit("tensor", lambda t, k=k, r=r, tp=tp:
                                   t.transpose(TPz[tp], ybuf[r][:, k * 128:(k + 1) * 128], ident[:, :]),
                                   waits=[toks["y"], ident_tok] + war.pop(("tpz", tp), []),
                                   incs=[(pe, 1)])
                            tp_tok = now(pe)
                            P.emit("scalar", lambda s, k=k, j=j, zr=zr, tp=tp:
                                   s.copy(zT[zr][:, k * CHUNK + j * 128: k * CHUNK + (j + 1) * 128],
                                          TPz[tp]),
                                   waits=[tp_tok] + (war.pop(("zslot", zr), [])
                                                     if (k == 0 and j == 0) else []),
                                   incs=[(act, 1)])
                            war[("tpz", tp)] = [now(act)]
                            last_evac = now(act)
                        war[("ybuf", r)] = [now(pe)]
                        war[("xr", r)] = [toks["y"], now(vec)]
                        vw = war.pop("vp", [])
                        for k in range(KC):
                            P.emit("tensor", lambda t, k=k, zr=zr, j=j:
                                   t.matmul(Vp, zT[zr][:, k * CHUNK + j * 128: k * CHUNK + (j + 1) * 128],
                                            wqkv_sb[:, k * 192 + 128: k * 192 + 192],
                                            start=(k == 0), stop=(k == KC - 1), skip_group_check=True),
                                   waits=([wqkv_tok, last_evac] + vw) if k == 0 else [],
                                   incs=[(pe, 1)] if k == KC - 1 else [])
                        vp_tok = now(pe)
                        P.emit("scalar", lambda s, i=i: s.copy(v_sb[:, i * 64:(i + 1) * 64], Vp),
                               waits=[vp_tok], incs=[(act, 1)])
                        vev = now(act)
                        kw = war.pop("kp", [])
                        for k in range(KC):
                            P.emit("tensor", lambda t, k=k, zr=zr, j=j:
                                   t.matmul(Kp, zT[zr][:, k * CHUNK + j * 128: k * CHUNK + (j + 1) * 128],
                                            wqkv_sb[:, k * 192 + 64: k * 192 + 128],
                                            start=(k == 0), stop=(k == KC - 1), skip_group_check=True),
                                   waits=kw if k == 0 else [],
                                   incs=[(pe, 1)] if k == KC - 1 else [])
                        kp_tok = now(pe)
                        P.emit("vector", lambda v, i=i: v.tensor_copy(k_sb[:, i * 64:(i + 1) * 64], Kp),
                               waits=[kp_tok], incs=[(vec, 1)])
                        war["kp"] = [now(vec)]
                        war["vp"] = [vev]
                    qkw = war.pop("qk", [])
                    for k in range(KC):
                        P.emit("tensor", lambda t, k=k, zr=zr:
                               t.matmul(QK[:, :], wqkv_sb[:, k * 192: k * 192 + 128],
                                        zT[zr][:, k * CHUNK:(k + 1) * CHUNK],
                                        start=(k == 0), stop=(k == KC - 1), skip_group_check=True),
                               waits=([wqkv_tok, last_evac] + qkw) if k == 0 else [],
                               incs=[(pe, 1)] if k == KC - 1 else [])
                    qk_tok = now(pe)
                    b = ch8 // 4
                    pos = (ch8 % 4) * CHUNK
                    P.emit("scalar", lambda s, b=b, pos=pos: s.activation(
                        qkT[0:64, b * SEQ + pos: b * SEQ + pos + CHUNK], QK[0:64, :], AF.Identity,
                        scale=f32(0.125)),
                           waits=[qk_tok], incs=[(act, 1)])
                    war["qk"] = [now(act)]
                    war[("zslot", zr)] = [now(pe)]
                qkT_tok = now(act)
                vsb_tok = now(act)
                ksb_tok = now(vec)

                for b in range(NB):
                    P.emit("gpsimd", lambda g: g.memset(vI[:, :], 1.0),
                           waits=[vsb_tok] + war.pop("vI", []), incs=[(gp, 1)])
                    vi0 = now(gp)
                    for jb in range(16):
                        P.emit("vector", lambda v, b=b, jb=jb: v.tensor_copy(
                            vI[:, jb * 128:(jb + 1) * 128].rearrange("p (h two) -> p h two", two=2)[:, :, 0:1],
                            v_sb[:, (b * 16 + jb) * 64:(b * 16 + jb + 1) * 64].rearrange("p (h o) -> p h o", o=1)),
                               waits=[vi0] if jb == 0 else [], incs=[(vec, 1)])
                    vI_tok = now(vec)

                    def load_rows(h, slot, waits_, b=b):
                        P.emit("gpsimd", lambda g_, h=h, b=b, slot=slot: g_.dma_start(
                            out=qc_r[0:1, slot * SEQ:(slot + 1) * SEQ],
                            in_=qkT[h:h + 1, b * SEQ:(b + 1) * SEQ]),
                               waits=waits_ + [qkT_tok], incs=[(dR[slot], 16)])
                        return now(dR[slot])

                    rowtok = {0: load_rows(0, 0, war.pop(("rows", 0), []))}
                    rowtok[1] = load_rows(1, 1, war.pop(("rows", 1), []))
                    war["vIfirst"] = [vI_tok]
                    pending = None

                    def flush_pending(p):
                        jb, e_tok, ndb_, njb_, h_, xw_, ic_ev = p
                        es = jb % 8
                        viw = war.pop("vIfirst", [])
                        P.emit("tensor", lambda t, h_=h_, jb=jb, es=es, ndb_=ndb_, njb_=njb_:
                               t.matmul(ND[ndb_],
                                        vI[:, jb * 128 + 2 * h_: jb * 128 + 2 * h_ + 2],
                                        Esb[es][:, :], start=(jb == 0), stop=(jb == njb_ - 1),
                                        skip_group_check=True),
                               waits=[e_tok] + viw + xw_, incs=[(pe, 1)])
                        war[("E", es)] = [now(pe)]
                        if ic_ev is not None:
                            nd_tok = now(pe)
                            ic = ic_ev
                            sg = (h_ * 4 + ic) % 4
                            P.emit("vector", lambda v, ndb_=ndb_, sg=sg: v.tensor_copy(
                                stg[sg][:, :], ND[ndb_]),
                                   waits=[nd_tok] + war.pop(("stg", sg), []), incs=[(vec, 1)])
                            st_tok = now(vec)
                            war[("nd", ndb_)] = [st_tok]
                            P.emit("gpsimd", lambda g_, h_=h_, ic=ic, sg=sg: g_.dma_start(
                                out=ndbuf[h_:h_ + 1, ic * 512:(ic + 1) * 512], in_=stg[sg][0:1, :]),
                                   waits=[st_tok] +
                                         (war.pop("ndbuf", []) if (h_ == 0 and ic == 0) else []),
                                   incs=[(dG[sg], 16)])
                            P.emit("gpsimd", lambda g_, h_=h_, ic=ic, sg=sg: g_.dma_start(
                                out=ndbuf[64 + h_:65 + h_, ic * 512:(ic + 1) * 512],
                                in_=stg[sg][1:2, :]),
                                   waits=[], incs=[(dG[sg], 16)])
                            war[("stg", sg)] = [now(dG[sg])]

                    for h in range(64):
                        slot = h % 2
                        rtok = rowtok.pop(h)
                        for ic in range(4):
                            ndb = ic % 2
                            ndw = war.pop(("nd", ndb), [])
                            njb = 4 * (ic + 1)
                            bqb = ic % 2
                            P.emit("tensor", lambda t, ic=ic, slot=slot, bqb=bqb:
                                   t.matmul(BQ[bqb][:, :], ones1[0:1, :],
                                            qc_r[0:1, slot * SEQ + ic * 512: slot * SEQ + (ic + 1) * 512],
                                            start=True, stop=True, skip_group_check=True),
                                   waits=[rtok, init_gp] + war.pop(("BQ", bqb), []), incs=[(pe, 1)])
                            bq_tok = now(pe)
                            for jb in range(njb):
                                es = jb % 8
                                ew = war.pop(("E", es), [])
                                ti_g = b * 16 + jb
                                P.emit("scalar", lambda s, es=es, bqb=bqb, ti_g=ti_g, h=h:
                                       s.activation(Esb[es][:, :], BQ[bqb][:, :], AF.Exp,
                                                    bias=f32(-2.0),
                                                    scale=k_sb[:, ti_g * 64 + h: ti_g * 64 + h + 1]),
                                       waits=[bq_tok, ksb_tok] + ew, incs=[(act, 1)])
                                e_tok = now(act)
                                if jb >= 4 * ic:
                                    m = jb - 4 * ic
                                    P.emit("vector", lambda v, es=es, m=m: v.tensor_tensor(
                                        Esb[es][:, :], Esb[es][:, :],
                                        masks[:, m * 512:(m + 1) * 512], ALU.mult),
                                           waits=[e_tok, mask_tok], incs=[(vec, 1)])
                                    e_tok = now(vec)
                                if pending is not None:
                                    flush_pending(pending)
                                pending = (jb, e_tok, ndb, njb, h,
                                           ndw if jb == 0 else [],
                                           ic if jb == njb - 1 else None)
                            war[("BQ", bqb)] = [now(act)]
                        if h + 2 < 64:
                            rowtok[h + 2] = load_rows(h + 2, slot, [now(pe)])
                    flush_pending(pending)
                    pending = None
                    scat_toks = [now(q) for q in dG]
                    P.emit("vector", lambda v: v.reciprocal(rden[:, :], ndbuf[64:128, :]),
                           waits=scat_toks + war.pop("rden", []), incs=[(vec, 1)])
                    P.emit("vector", lambda v: v.tensor_mul(oT[:, :], ndbuf[0:64, :], rden[:, :]),
                           waits=[now(vec)] + war.pop("oT", []), incs=[(vec, 1)])
                    oT_tok = now(vec)
                    war["ndbuf"] = [oT_tok]
                    war["rden"] = [oT_tok]
                    for j in range(16):
                        i = b * 16 + j
                        r = i % 2
                        optoks = []
                        for cc in range(2):
                            opw = war.pop(("OP", cc), [])
                            P.emit("tensor", lambda t, j=j, cc=cc:
                                   t.matmul(OP[cc][:, :], oT[:, j * 128:(j + 1) * 128],
                                            wout_sb[:, cc * 512:(cc + 1) * 512],
                                            start=True, stop=True, skip_group_check=True),
                                   waits=[oT_tok, wout_tok] + opw, incs=[(pe, 1)])
                            optoks.append(now(pe))
                        P.emit("sync", lambda sy, i=i, r=r: sy.dma_start(
                            out=xr[r][:, :], in_=src_dram[i * 128:(i + 1) * 128, :]),
                               waits=war.pop(("xr", r), []), incs=[(dXs[r], 16)])
                        ld = now(dXs[r])
                        ybw = war.pop(("ybuf", r), [])
                        for cc in range(2):
                            P.emit("vector", lambda v, r=r, cc=cc: v.tensor_tensor(
                                ybuf[r][:, cc * 512:(cc + 1) * 512], xr[r][:, cc * 512:(cc + 1) * 512],
                                OP[cc][:, :], ALU.add),
                                   waits=[ld, optoks[cc]] + (ybw if cc == 0 else []),
                                   incs=[(vec, 1)])
                            war[("OP", cc)] = [now(vec)]
                        stt_tok = now(vec)
                        war[("xr", r)] = [stt_tok]
                        P.emit("sync", lambda sy, i=i, r=r: sy.dma_start(
                            out=dst_dram[i * 128:(i + 1) * 128, :], in_=ybuf[r][:, :]),
                               waits=[stt_tok], incs=[(dYs[r], 16)])
                        war[("ybuf", r)] = [now(dYs[r])]
                    war["oT"] = [now(pe)]
                    war["vI"] = [now(pe)]
                    war[("rows", 0)] = [now(pe)]
                    war[("rows", 1)] = [now(pe)]
            engine_barrier()

        if "F1" in phases:
            ffn_phase(x_d, xs_d, "f1")
            a_src = xs_d
        else:
            a_src = x_d
        if "A" in phases:
            attn_phase(a_src, xs_d)
            f2_src = xs_d
        else:
            f2_src = a_src
        if "F2" in phases:
            ffn_phase(f2_src, out_d, "f2")
        else:
            prev = None
            for i in range(NTILES):
                P.emit("sync", lambda sy, i=i: sy.dma_start(out=xr[0][:, :],
                                                            in_=f2_src[i * 128:(i + 1) * 128, :]),
                       waits=[prev] if prev else [], incs=[(dXs[0], 16)])
                ld = now(dXs[0])
                P.emit("sync", lambda sy, i=i: sy.dma_start(out=out_d[i * 128:(i + 1) * 128, :],
                                                            in_=xr[0][:, :]),
                       waits=[ld], incs=[(dYs[0], 16)])
                prev = now(dYs[0])

        P.emit("sync", lambda sy: sy.nop(), waits=[now(dYs[0]), now(dYs[1])])

        with ExitStack() as sctx:
            for s in all_sems:
                s.h = sctx.enter_context(nc.semaphore(s.name))
            block = sctx.enter_context(nc.Block())
            block.tensor(lambda t: P.run("tensor", t))
            block.vector(lambda v: P.run("vector", v))
            block.scalar(lambda s: P.run("scalar", s))
            block.gpsimd(lambda g: P.run("gpsimd", g))
            block.sync(lambda sy: P.run("sync", sy))
    return nc


def kernel(L0_attn_x, L0_attn_w_qkv, L0_attn_w_out, L0_ffn_gate_w, L0_ffn_expert_bias,
           L0_ffn_w_up, L0_ffn_w_down, L0_ffn_shared_w_up, L0_ffn_shared_w_down):
    x = np.ascontiguousarray(np.asarray(L0_attn_x, dtype=np.float32))
    wqkvT = np.ascontiguousarray(np.asarray(L0_attn_w_qkv, np.float32).T)
    woutT = np.ascontiguousarray(np.asarray(L0_attn_w_out, np.float32).T)
    gwT = np.ascontiguousarray(np.asarray(L0_ffn_gate_w, np.float32).T)
    biasbc = np.ascontiguousarray(
        np.tile(np.asarray(L0_ffn_expert_bias, np.float32)[None, :], (128, 1)))
    up = np.asarray(L0_ffn_w_up, np.float32)
    dn = np.asarray(L0_ffn_w_down, np.float32)
    sup = np.asarray(L0_ffn_shared_w_up, np.float32)
    sdn = np.asarray(L0_ffn_shared_w_down, np.float32)
    upT = np.ascontiguousarray(np.concatenate([up.transpose(0, 2, 1), sup.T[None]], axis=0))
    dnT = np.ascontiguousarray(np.concatenate([dn.transpose(0, 2, 1), sdn.T[None]], axis=0))

    nc = build_nc()
    in_maps = []
    for c in range(NCORES):
        xs = np.ascontiguousarray(x[2 * c:2 * c + 2].reshape(T, D))
        in_maps.append(dict(x=xs, wqkvT=wqkvT, woutT=woutT, gwT=gwT, biasbc=biasbc,
                            upT=upT, dnT=dnT))
    res = run_bass_kernel_spmd(nc, in_maps, core_ids=list(range(NCORES)))
    out = np.concatenate([np.asarray(res.results[c]["out"]).reshape(2, SEQ, D)
                          for c in range(NCORES)], axis=0)
    return out.astype(np.float32)
